# version 05
# baseline (speedup 1.0000x reference)
import math
import numpy as np
import ml_dtypes
import concourse.bass as bass
import concourse.mybir as mybir
from concourse.bass_utils import run_bass_kernel_spmd
from contextlib import ExitStack

F32 = mybir.dt.float32
BF16 = mybir.dt.bfloat16
AF = mybir.ActivationFunctionType
ALU = mybir.AluOpType
BF = ml_dtypes.bfloat16

B, T, D, H, HD = 8, 4096, 1024, 16, 64
FH = 4096
QB, NQB = 512, T // 512
P3B, NP3 = 256, T // 256
LN_EPS = 1e-5
ABS_EPS = 1e-6

ENGS = ("sync", "gpsimd", "tensor", "scalar", "vector")

_CONSTS = None


def _host_consts():
    global _CONSTS
    if _CONSTS is not None:
        return _CONSTS
    dim = D // 2
    pos = np.arange(dim, dtype=np.float64)
    freqs = 1.0 / 10000.0 ** (pos / dim)
    active = int(np.sum(freqs * 2 * math.pi * T >= 1.0))
    active = min(active, dim - D // 4)
    f = freqs[:active]
    i = np.arange(T, dtype=np.float64)
    th = np.outer(f, i)
    cs, sn = np.cos(th), np.sin(th)
    u = np.empty((8 * 64, T), dtype=np.float64)
    w = np.empty((8 * 64, T), dtype=np.float64)
    for h in range(8):
        for c in range(32):
            g = 32 * h + c
            u[64 * h + 2 * c] = cs[g]
            u[64 * h + 2 * c + 1] = sn[g]
            w[64 * h + 2 * c] = cs[g] - sn[g]
            w[64 * h + 2 * c + 1] = cs[g] + sn[g]
    d = np.arange(-(T - 1), T, dtype=np.float64)
    cshift = []
    for h in range(8):
        fh = f[32 * h:32 * h + 32]
        pv = np.cos(np.outer(d, fh)) + np.sin(np.outer(d, fh))
        cshift.append(float(pv.sum(axis=1).max()) + 12.0)
    mask = np.triu(np.ones((128, 128), dtype=np.float64))
    _CONSTS = dict(
        u_tab=u.astype(BF), wb_tab=w.astype(BF), cshift=cshift,
        mask_bf=mask.astype(BF),
        ident_f32=np.eye(128, dtype=np.float32),
        ones_f32=np.ones((128, 128), dtype=np.float32),
        ones_bf=np.ones((128, 128), dtype=BF),
    )
    return _CONSTS


class Sched:

    def __init__(self, nc, sems):
        self.nc = nc
        self.sems = sems
        self.cnt = {e: 0 for e in ENGS}
        self.q = {e: [] for e in ENGS}
        self.lastwait = {}

    def wait(self, eng, src, val):
        if val is None or val <= 0:
            return
        key = (eng, src)
        if self.lastwait.get(key, -1) >= val:
            return
        self.lastwait[key] = val
        h = self.sems[src]
        self.q[eng].append(lambda e, h=h, v=val: e.wait_ge(h, v))

    def emit(self, eng, fn, inc=0):
        if inc:
            self.cnt[eng] += inc
            h = self.sems[eng]

            def g(e, fn=fn, h=h, inc=inc):
                fn(e).then_inc(h, inc)

            self.q[eng].append(g)
        else:
            self.q[eng].append(fn)
        return self.cnt[eng]

    def val(self, eng):
        return self.cnt[eng]

    def fence(self, eng):
        self.wait(eng, eng, self.cnt[eng])
        return self.emit(eng, lambda e: e.nop(), 1)

    def chain(self, eng, fn):
        wm = self.emit(eng, fn, 1)
        self.wait(eng, eng, wm)
        return wm

    def flush(self, block):
        for eng in ENGS:
            ops = self.q[eng]
            if not ops:
                continue

            def body(e, ops=ops):
                for f in ops:
                    f(e)

            getattr(block, eng)(body)
            self.q[eng] = []

    def barrier(self):
        snap = dict(self.cnt)
        for eng in ENGS:
            for src in ENGS:
                if src != eng:
                    self.wait(eng, src, snap[src])


def emit_layernorm(s, rsb, out_ap, sc, c0, junk, nfree):
    inv = 1.0 / nfree
    v_s1 = s.emit("vector", lambda e, r=rsb, o=sc[:, c0:c0 + 1]:
                  e.reduce_sum(o, r, axis=mybir.AxisListType.X), 1)
    sqw = s.chain("scalar", lambda e, r=rsb, j=junk:
                  e.activation(j, r, AF.Square))
    s.wait("vector", "scalar", sqw)
    v_s2 = s.emit("vector", lambda e, j=junk, o=sc[:, c0 + 1:c0 + 2]:
                  e.reduce_sum(o, j, axis=mybir.AxisListType.X), 1)
    s.wait("scalar", "vector", v_s2)
    s.wait("scalar", "vector", v_s1)
    s.chain("scalar", lambda e, i=sc[:, c0:c0 + 1], o=sc[:, c0 + 2:c0 + 3]:
            e.activation(o, i, AF.Copy, scale=inv))
    s.chain("scalar", lambda e, i=sc[:, c0 + 2:c0 + 3], o=sc[:, c0 + 3:c0 + 4]:
            e.activation(o, i, AF.Copy, scale=-1.0))
    s.chain("scalar", lambda e, i=sc[:, c0 + 2:c0 + 3], o=sc[:, c0 + 4:c0 + 5],
            sca=sc[:, c0 + 3:c0 + 4]:
            e.activation(o, i, AF.Identity, scale=sca))
    s.chain("scalar", lambda e, i=sc[:, c0 + 4:c0 + 5], o=sc[:, c0 + 5:c0 + 6]:
            e.activation(o, i, AF.Identity, bias=np.float32(LN_EPS)))
    s.chain("scalar", lambda e, i=sc[:, c0 + 1:c0 + 2], o=sc[:, c0 + 6:c0 + 7],
            b=sc[:, c0 + 5:c0 + 6]:
            e.activation(o, i, AF.Ln, scale=inv, bias=b))
    s.chain("scalar", lambda e, i=sc[:, c0 + 6:c0 + 7], o=sc[:, c0 + 5:c0 + 6]:
            e.activation(o, i, AF.Exp, scale=-0.5))
    s.chain("scalar", lambda e, i=sc[:, c0 + 5:c0 + 6], o=sc[:, c0 + 6:c0 + 7],
            sca=sc[:, c0 + 3:c0 + 4]:
            e.activation(o, i, AF.Identity, scale=sca))
    wm = s.emit("scalar", lambda e, r=rsb, o=out_ap,
                sca=sc[:, c0 + 5:c0 + 6], b=sc[:, c0 + 6:c0 + 7]:
                e.activation(o, r, AF.Identity, scale=sca, bias=b), 1)
    return wm, v_s1


def build(dbg=None):
    C = _host_consts()
    nc = bass.Bass("TRN2", target_bir_lowering=False, debug=False)

    x_d = nc.dram_tensor("x", [T, D], F32, kind="ExternalInput").ap()
    wqkv_d = nc.dram_tensor("w_qkv", [3 * D, D], F32, kind="ExternalInput").ap()
    bqkv_d = nc.dram_tensor("b_qkv", [3 * D], F32, kind="ExternalInput").ap()
    wproj_d = nc.dram_tensor("w_proj", [D, D], F32, kind="ExternalInput").ap()
    bproj_d = nc.dram_tensor("b_proj", [D], F32, kind="ExternalInput").ap()
    pos_d = nc.dram_tensor("pos_ind", [D // 4, 2], F32, kind="ExternalInput").ap()
    wfc_d = nc.dram_tensor("W_fc", [FH, D], F32, kind="ExternalInput").ap()
    lnw_d = nc.dram_tensor("ln_w", [FH], F32, kind="ExternalInput").ap()
    lnb_d = nc.dram_tensor("ln_b", [FH], F32, kind="ExternalInput").ap()
    wpr_d = nc.dram_tensor("W_pr", [D, FH], F32, kind="ExternalInput").ap()
    ut_d = nc.dram_tensor("u_tab", [512, T], BF16, kind="ExternalInput").ap()
    wt_d = nc.dram_tensor("wb_tab", [512, T], BF16, kind="ExternalInput").ap()
    idf_d = nc.dram_tensor("ident_f32", [128, 128], F32, kind="ExternalInput").ap()
    onf_d = nc.dram_tensor("ones_f32", [128, 128], F32, kind="ExternalInput").ap()
    onb_d = nc.dram_tensor("ones_bf", [128, 128], BF16, kind="ExternalInput").ap()
    msk_d = nc.dram_tensor("mask_bf", [128, 128], BF16, kind="ExternalInput").ap()
    out_d = nc.dram_tensor("out", [T, D], F32, kind="ExternalOutput").ap()

    def scratch(name, shape, dt):
        kind = "ExternalOutput" if (dbg and name in dbg) else "Internal"
        return nc.dram_tensor(name, shape, dt, kind=kind).ap()

    xbf_d = scratch("x_bf", [T, D], BF16)
    wqkvbf_d = scratch("wqkv_bf", [3 * D, D], BF16)
    wprojbf_d = scratch("wproj_bf", [D, D], BF16)
    wfcq_d = scratch("wfc_q", [FH, D], BF16)
    wprq_d = scratch("wpr_q", [D, FH], BF16)
    ksp_d = scratch("k_sp", [12, 128, T], BF16)
    qsp_d = scratch("q_sp", [NQB, 1024, QB], BF16)
    xn32_d = scratch("xnew32", [T, D], F32)
    xnbf_d = scratch("xnewbf", [T, D], BF16)

    dbg_stop = dbg.get("stop") if dbg else None

    with ExitStack() as S0:
        e0 = S0.enter_context
        sems = {eng: e0(nc.semaphore(f"sem_{eng}")) for eng in ENGS}
        idf = e0(nc.sbuf_tensor("idf", [128, 128], F32))
        onf = e0(nc.sbuf_tensor("onf", [128, 128], F32))
        onb = e0(nc.sbuf_tensor("onb", [128, 128], BF16))
        msk = e0(nc.sbuf_tensor("msk", [128, 128], BF16))
        bqc = e0(nc.sbuf_tensor("bqc", [128, 24], F32))
        bq8 = e0(nc.sbuf_tensor("bq8", [128, 8], F32))
        c8b = e0(nc.sbuf_tensor("c8b", [128, 8], F32))
        lnwc = e0(nc.sbuf_tensor("lnwc", [128, 32], F32))
        lnbc = e0(nc.sbuf_tensor("lnbc", [128, 32], F32))
        bvbc = e0(nc.sbuf_tensor("bvbc", [128, D], F32))
        bpbc = e0(nc.sbuf_tensor("bpbc", [128, D], F32))
        block = e0(nc.Block())
        s = Sched(nc, sems)

        with ExitStack() as SP:
            ee = SP.enter_context
            stg32 = [ee(nc.sbuf_tensor(f"stg32_{i}", [128, FH], F32)) for i in range(2)]
            stgbf = [ee(nc.sbuf_tensor(f"stgbf_{i}", [128, FH], BF16)) for i in range(2)]
            tpos = ee(nc.sbuf_tensor("tpos", [128, FH], BF16))
            tneg = ee(nc.sbuf_tensor("tneg", [128, FH], BF16))
            wqb = [ee(nc.sbuf_tensor(f"wqb_{i}", [128, FH], BF16)) for i in range(2)]
            sm24 = ee(nc.sbuf_tensor("sm24", [128, 384], F32))
            bvrow = ee(nc.sbuf_tensor("bvrow", [1, D], F32))
            bprow = ee(nc.sbuf_tensor("bprow", [1, D], F32))
            posr = ee(nc.sbuf_tensor("posr", [8, 64], F32))
            posc = ee(nc.sbuf_tensor("posc", [128, 2], F32))
            c8row = ee(nc.sbuf_tensor("c8row", [1, 8], F32))
            acc = ee(nc.sbuf_tensor("acc", [128, 2], F32))
            red = ee(nc.sbuf_tensor("red", [128, 32], F32))
            absb = [ee(nc.sbuf_tensor(f"absb{i}", [128, FH], F32)) for i in range(2)]
            gam = ee(nc.sbuf_tensor("gam", [1, 2], F32))
            thr = ee(nc.sbuf_tensor("thr", [128, 2], F32))
            pA = ee(nc.psum_tensor("pA", [128, 512], F32))
            pB = ee(nc.psum_tensor("pB", [128, 512], F32))

            for dst, src in ((idf, idf_d), (onf, onf_d), (onb, onb_d), (msk, msk_d)):
                s.emit("sync", lambda e, d=dst, i=src: e.dma_start(out=d[:], in_=i), 16)
            w_ld0 = s.emit("sync", lambda e: e.dma_start(
                out=sm24[0:24, 0:128], in_=bqkv_d.rearrange("(a b) -> a b", b=128)), 16)
            s.emit("sync", lambda e: e.dma_start(
                out=sm24[0:32, 128:256], in_=lnw_d.rearrange("(a b) -> a b", b=128)), 16)
            w_ld2 = s.emit("sync", lambda e: e.dma_start(
                out=sm24[0:32, 256:384], in_=lnb_d.rearrange("(a b) -> a b", b=128)), 16)
            w_ld3 = s.emit("sync", lambda e: e.dma_start(
                out=posr[:], in_=pos_d.rearrange("(a x) c -> a (x c)", a=8)), 16)
            s.emit("sync", lambda e: e.dma_start(
                out=bvrow[:], in_=bqkv_d[2048:3072].rearrange("(a b) -> a b", a=1)), 16)
            w_ld5 = s.emit("sync", lambda e: e.dma_start(
                out=bprow[:], in_=bproj_d.rearrange("(a b) -> a b", a=1)), 16)

            if dbg_stop == "p0s1":
                s.barrier()
                s.flush(block)
                return nc
            s.wait("tensor", "sync", w_ld0)
            t1 = s.emit("tensor", lambda e: e.transpose(
                pA[:, 0:24], sm24[0:24, 0:128], idf[0:24, 0:24]), 1)
            s.wait("tensor", "sync", w_ld2)
            s.emit("tensor", lambda e: e.transpose(
                pB[:, 0:32], sm24[0:32, 128:256], idf[0:32, 0:32]), 1)
            t3 = s.emit("tensor", lambda e: e.transpose(
                pB[:, 32:64], sm24[0:32, 256:384], idf[0:32, 0:32]), 1)
            s.wait("scalar", "tensor", t1)
            s.emit("scalar", lambda e: e.activation(bqc[:], pA[:, 0:24], AF.Copy), 1)
            s.emit("scalar", lambda e: e.activation(bq8[:], pA[:, 0:8], AF.Copy, scale=0.125), 1)
            s.wait("scalar", "tensor", t3)
            s.emit("scalar", lambda e: e.activation(lnwc[:], pB[:, 0:32], AF.Copy), 1)
            c_ev = s.emit("scalar", lambda e: e.activation(lnbc[:], pB[:, 32:64], AF.Copy), 1)

            if dbg_stop == "p0s2":
                s.barrier()
                s.flush(block)
                return nc
            s.wait("vector", "sync", w_ld3)
            v1 = s.emit("vector", lambda e: e.reduce_sum(
                posc[0:8, 0:1], posr[:], axis=mybir.AxisListType.X), 1)
            s.wait("tensor", "scalar", c_ev)
            s.wait("tensor", "vector", v1)
            t4 = s.emit("tensor", lambda e: e.transpose(
                pA[0:1, 0:8], posc[0:8, 0:1], idf[0:8, 0:8]), 1)
            s.wait("scalar", "tensor", t4)
            s.emit("scalar", lambda e: e.activation(c8row[:], pA[0:1, 0:8], AF.Copy), 1)
            c_ev2 = s.fence("scalar")
            s.wait("tensor", "scalar", c_ev2)
            t5 = s.emit("tensor", lambda e: e.matmul(
                pB[:, 0:8], onf[0:1, 0:128], c8row[:], start=True, stop=True), 1)
            s.wait("scalar", "tensor", t5)
            c8done = s.emit("scalar", lambda e: e.activation(
                c8b[:], pB[:, 0:8], AF.Identity, bias=np.float32(-12.0)), 1)

            if dbg_stop == "p0s3":
                s.barrier()
                s.flush(block)
                return nc
            s.wait("tensor", "sync", w_ld5)
            s.wait("tensor", "scalar", c8done)
            bsv = 0
            for half in range(2):
                t6 = s.emit("tensor", lambda e, h=half: e.matmul(
                    pA[:, 0:512], onf[0:1, 0:128], bvrow[0:1, h * 512:(h + 1) * 512],
                    start=True, stop=True), 1)
                s.wait("scalar", "tensor", t6)
                sv = s.emit("scalar", lambda e, h=half: e.activation(
                    bvbc[:, h * 512:(h + 1) * 512], pA[:, 0:512], AF.Copy), 1)
                s.wait("tensor", "scalar", sv)
                t7 = s.emit("tensor", lambda e, h=half: e.matmul(
                    pA[:, 0:512], onf[0:1, 0:128], bprow[0:1, h * 512:(h + 1) * 512],
                    start=True, stop=True), 1)
                s.wait("scalar", "tensor", t7)
                bsv = s.emit("scalar", lambda e, h=half: e.activation(
                    bpbc[:, h * 512:(h + 1) * 512], pA[:, 0:512], AF.Copy), 1)
                s.wait("tensor", "scalar", bsv)

            if dbg_stop == "p0s":
                s.barrier()
                s.flush(block)
                return nc

            cast_jobs = []
            for i in range(24):
                cast_jobs.append((wqkv_d[128 * i:128 * (i + 1), :], wqkvbf_d[128 * i:128 * (i + 1), :], D))
            for i in range(8):
                cast_jobs.append((wproj_d[128 * i:128 * (i + 1), :], wprojbf_d[128 * i:128 * (i + 1), :], D))
            for i in range(32):
                cast_jobs.append((x_d[128 * i:128 * (i + 1), :], xbf_d[128 * i:128 * (i + 1), :], D))
            ld_w, cast_w, st_w = [], [], []
            for n, (src, dst, width) in enumerate(cast_jobs):
                par = n % 2
                if n >= 2:
                    s.wait("sync", "scalar", cast_w[n - 2])
                ld_w.append(s.emit("sync", lambda e, sc=src, p=par, w=width:
                                   e.dma_start(out=stg32[p][:, 0:w], in_=sc), 16))
                s.wait("scalar", "sync", ld_w[n])
                if n >= 2:
                    s.wait("scalar", "gpsimd", st_w[n - 2])
                cast_w.append(s.emit("scalar", lambda e, p=par, w=width:
                                     e.activation(stgbf[p][:, 0:w], stg32[p][:, 0:w], AF.Copy), 1))
                s.wait("gpsimd", "scalar", cast_w[n])
                st_w.append(s.emit("gpsimd", lambda e, d=dst, p=par, w=width:
                                   e.dma_start(out=d, in_=stgbf[p][:, 0:w]), 16))

            if dbg_stop == "p0c":
                s.barrier()
                s.flush(block)
                return nc

            s.wait("sync", "gpsimd", st_w[-1])
            wb_st = []
            for h in range(8):
                par = h % 2
                if h >= 2:
                    s.wait("sync", "gpsimd", wb_st[h - 2])
                wl = s.emit("sync", lambda e, h=h, p=par: e.dma_start(
                    out=stgbf[p][0:64, 0:T], in_=wt_d[64 * h:64 * h + 64, :]), 16)
                s.wait("gpsimd", "sync", wl)
                wb_st.append(s.emit("gpsimd", lambda e, h=h, p=par: e.dma_start(
                    out=ksp_d[h, 64:128, :], in_=stgbf[p][0:64, 0:T]), 16))

            if dbg_stop == "p0w":
                s.barrier()
                s.flush(block)
                return nc

            def ternarize(src_d, dst_d, ntile, width, gcol):
                s.wait("sync", "scalar", s.val("scalar"))
                lds, abss = [], []
                vred = 0
                for i in range(ntile):
                    par = i % 2
                    if i >= 2:
                        s.wait("sync", "scalar", abss[i - 2])
                    lds.append(s.emit("sync", lambda e, i=i, p=par, w=width:
                                      e.dma_start(out=stg32[p][:, 0:w],
                                                  in_=src_d[128 * i:128 * (i + 1), 0:w]), 16))
                    s.wait("scalar", "sync", lds[i])
                    if i >= 2:
                        s.wait("scalar", "vector", vreds[i - 2] if False else vred_l[i - 2])
                    abss.append(s.emit("scalar", lambda e, i=i, p=par, w=width:
                                       e.activation(absb[p][:, 0:w], stg32[p][:, 0:w], AF.Abs), 1))
                    s.wait("vector", "scalar", abss[i])
                    if i == 0:
                        vred_l = []
                    vred_l.append(s.emit("vector", lambda e, i=i, p=par, w=width:
                                         e.reduce_sum(red[:, i:i + 1], absb[p][:, 0:w],
                                                      axis=mybir.AxisListType.X), 1))
                vred = vred_l[-1]
                s.wait("tensor", "vector", vred)
                s.wait("tensor", "scalar", s.val("scalar"))
                tg = s.emit("tensor", lambda e: e.matmul(
                    pB[0:1, 0:ntile], onf[:, 0:1], red[:, 0:ntile], start=True, stop=True), 1)
                s.wait("vector", "tensor", tg)
                vgg = s.emit("vector", lambda e: e.reduce_sum(
                    gam[0:1, gcol:gcol + 1], pB[0:1, 0:ntile],
                    axis=mybir.AxisListType.X), 1)
                s.wait("scalar", "vector", vgg)
                s.emit("scalar", lambda e: e.activation(
                    gam[0:1, gcol:gcol + 1], gam[0:1, gcol:gcol + 1], AF.Identity,
                    scale=1.0 / (ntile * 128 * width), bias=np.float32(ABS_EPS)), 1)
                sg = s.fence("scalar")
                s.wait("tensor", "scalar", sg)
                tb = s.emit("tensor", lambda e: e.matmul(
                    pB[:, 0:1], onf[0:1, 0:128], gam[0:1, gcol:gcol + 1], start=True, stop=True), 1)
                s.wait("scalar", "tensor", tb)
                s.emit("scalar", lambda e: e.activation(
                    thr[:, 0:1], pB[:, 0:1], AF.Copy, scale=0.5), 1)
                s.emit("scalar", lambda e: e.activation(
                    thr[:, 1:2], pB[:, 0:1], AF.Copy, scale=-0.5), 1)
                sth = s.fence("scalar")
                s.wait("sync", "scalar", sth)
                vq, stq = [], []
                for i in range(ntile):
                    par = i % 2
                    if i >= 2:
                        s.wait("sync", "vector", vq[i - 2])
                    ld2 = s.emit("sync", lambda e, i=i, p=par, w=width:
                                 e.dma_start(out=stg32[p][:, 0:w],
                                             in_=src_d[128 * i:128 * (i + 1), 0:w]), 16)
                    s.wait("vector", "sync", ld2)
                    if i >= 2:
                        s.wait("vector", "gpsimd", stq[i - 2])
                    s.emit("vector", lambda e, p=par, w=width: e.tensor_scalar(
                        tpos[:, 0:w], stg32[p][:, 0:w], thr[:, 0:1], None, ALU.is_gt), 1)
                    s.emit("vector", lambda e, p=par, w=width: e.tensor_scalar(
                        tneg[:, 0:w], stg32[p][:, 0:w], thr[:, 1:2], None, ALU.is_lt), 1)
                    vq.append(s.emit("vector", lambda e, p=par, w=width: e.tensor_sub(
                        wqb[p][:, 0:w], tpos[:, 0:w], tneg[:, 0:w]), 1))
                    s.wait("gpsimd", "vector", vq[i])
                    stq.append(s.emit("gpsimd", lambda e, i=i, p=par, w=width: e.dma_start(
                        out=dst_d[128 * i:128 * (i + 1), 0:w], in_=wqb[p][:, 0:w]), 16))

            ternarize(wfc_d, wfcq_d, 32, D, 0)
            ternarize(wpr_d, wprq_d, 8, FH, 1)

            s.barrier()
            with nc.named_scope("P0"):
                s.flush(block)

        if dbg_stop == "p0":
            s.wait("sync", "gpsimd", s.val("gpsimd"))
            s.flush(block)
            return nc

        with ExitStack() as S12:
            e12 = S12.enter_context
            vaug = e12(nc.sbuf_tensor("vaug", [128, 32 * 1040], BF16))

            with ExitStack() as SP:
                ee = SP.enter_context
                wqkvT = ee(nc.sbuf_tensor("wqkvT", [128, 8 * 3072], BF16))
                xfm = [ee(nc.sbuf_tensor(f"xfm{i}", [128, 8 * 512], BF16)) for i in range(2)]
                qkst = [ee(nc.sbuf_tensor(f"qkst{i}", [128, 512], BF16)) for i in range(2)]
                pk = [ee(nc.psum_tensor(f"pk{i}", [128, 512], F32)) for i in range(2)]
                pv = [ee(nc.psum_tensor(f"pv{i}", [128, 512], F32)) for i in range(2)]

                s.emit("vector", lambda e: e.memset(vaug[:], 1.0), 1)

                wq_ld = 0
                for dc in range(8):
                    for fb in range(6):
                        wq_ld = s.emit("sync", lambda e, dc=dc, fb=fb: e.dma_start_transpose(
                            wqkvT[:, dc * 3072 + fb * 512: dc * 3072 + (fb + 1) * 512],
                            wqkvbf_d[fb * 512:(fb + 1) * 512, dc * 128:(dc + 1) * 128]), 16)

                evac_w = {}
                vev_w = {}
                st_q = {}
                blk_done = {}
                for j in range(NQB):
                    pj = j % 2
                    if j >= 2:
                        s.wait("sync", "tensor", blk_done[j - 2])
                    xf_ld = 0
                    for dc in range(8):
                        xf_ld = s.emit("sync", lambda e, j=j, dc=dc, p=pj: e.dma_start_transpose(
                            xfm[p][:, dc * 512:(dc + 1) * 512],
                            xbf_d[j * 512:(j + 1) * 512, dc * 128:(dc + 1) * 128]), 16)
                    s.wait("tensor", "sync", max(xf_ld, wq_ld))
                    for part, fbase in ((0, 0), (1, 1024)):
                        for fc in range(8):
                            par = fc % 2
                            s.wait("tensor", "scalar", evac_w.get(par, 0))
                            last = None
                            for dc in range(8):
                                last = s.emit("tensor",
                                              lambda e, dc=dc, fc=fc, p=pj, q=par, fb=fbase:
                                              e.matmul(pk[q][:, 0:512],
                                                       wqkvT[:, dc * 3072 + fb + fc * 128:
                                                             dc * 3072 + fb + (fc + 1) * 128],
                                                       xfm[p][:, dc * 512:(dc + 1) * 512],
                                                       start=(dc == 0), stop=(dc == 7)),
                                              1 if dc == 7 else 0)
                            s.wait("scalar", "tensor", last)
                            s.wait("scalar", "gpsimd", st_q.get(par, 0))
                            if part == 0:
                                evac_w[par] = s.emit("scalar", lambda e, q=par, fc=fc:
                                                     e.activation(qkst[q][:], pk[q][:, 0:512],
                                                                  AF.Identity, scale=0.125,
                                                                  bias=bq8[:, fc:fc + 1]), 1)
                                s.wait("gpsimd", "scalar", evac_w[par])
                                st_q[par] = s.emit("gpsimd", lambda e, j=j, fc=fc, q=par:
                                                   e.dma_start(out=qsp_d[j, fc * 128:(fc + 1) * 128, :],
                                                               in_=qkst[q][:]), 16)
                            else:
                                evac_w[par] = s.emit("scalar", lambda e, q=par, fc=fc:
                                                     e.activation(qkst[q][:], pk[q][:, 0:512],
                                                                  AF.Identity,
                                                                  bias=bqc[:, 8 + fc:9 + fc]), 1)
                                s.wait("gpsimd", "scalar", evac_w[par])
                                if fc < 4:
                                    s.emit("gpsimd", lambda e, j=j, fc=fc, q=par: e.dma_start(
                                        out=ksp_d[2 * fc, 0:64, j * 512:(j + 1) * 512],
                                        in_=qkst[q][0:64, :]), 16)
                                    st_q[par] = s.emit("gpsimd", lambda e, j=j, fc=fc, q=par:
                                                       e.dma_start(
                                        out=ksp_d[2 * fc + 1, 0:64, j * 512:(j + 1) * 512],
                                        in_=qkst[q][64:128, :]), 16)
                                else:
                                    st_q[par] = s.emit("gpsimd", lambda e, j=j, fc=fc, q=par:
                                                       e.dma_start(
                                        out=ksp_d[8 + fc - 4, :, j * 512:(j + 1) * 512],
                                        in_=qkst[q][:]), 16)
                    lastv = None
                    for mt in range(4):
                        for ncol in range(2):
                            par = (2 * mt + ncol) % 2
                            s.wait("tensor", "vector", vev_w.get(par, 0))
                            for dc in range(8):
                                lastv = s.emit("tensor",
                                               lambda e, dc=dc, mt=mt, nn=ncol, p=pj, q=par:
                                               e.matmul(pv[q][:, 0:512],
                                                        xfm[p][:, dc * 512 + mt * 128:
                                                               dc * 512 + (mt + 1) * 128],
                                                        wqkvT[:, dc * 3072 + 2048 + nn * 512:
                                                              dc * 3072 + 2048 + (nn + 1) * 512],
                                                        start=(dc == 0), stop=(dc == 7)),
                                               1 if dc == 7 else 0)
                            s.wait("vector", "tensor", lastv)
                            base = (4 * j + mt) * 1040 + ncol * 8 * 65
                            vev_w[par] = s.emit(
                                "vector",
                                lambda e, b=base, q=par, nn=ncol: e.tensor_tensor(
                                    vaug[:, b:b + 520].rearrange("p (h c) -> p h c", c=65)[:, :, 0:64],
                                    pv[q][:, 0:512].rearrange("p (h c) -> p h c", c=64),
                                    bvbc[:, nn * 512:(nn + 1) * 512].rearrange("p (h c) -> p h c", c=64),
                                    ALU.add), 1)
                    blk_done[j] = s.val("tensor")

                s.barrier()
                with nc.named_scope("P1"):
                    s.flush(block)

            if dbg_stop == "p1":
                if dbg and "vaug_dump" in dbg:
                    vd = nc.dram_tensor("vaug_dump", [128, 32 * 1040], BF16,
                                        kind="ExternalOutput").ap()
                    s.emit("gpsimd", lambda e: e.dma_start(out=vd, in_=vaug[:]), 16)
                s.wait("sync", "gpsimd", s.val("gpsimd"))
                s.flush(block)
                return nc

            with ExitStack() as SP:
                ee = SP.enter_context
                wprojT = ee(nc.sbuf_tensor("wprojT", [128, 8 * 1024], BF16))
                kh = [ee(nc.sbuf_tensor(f"kh{i}", [128, T], BF16)) for i in range(2)]
                qa = ee(nc.sbuf_tensor("qa", [128, 12 * 512], BF16))
                probs = [ee(nc.sbuf_tensor(f"probs{i}", [128, 1024], BF16)) for i in range(3)]
                afm = ee(nc.sbuf_tensor("afm", [128, 8 * 512], BF16))
                xres = ee(nc.sbuf_tensor("xres", [128, 4 * 1024], F32))
                rsb = [ee(nc.sbuf_tensor(f"rsb{i}", [128, 1024], F32)) for i in range(2)]
                den = [ee(nc.sbuf_tensor(f"den{i}", [1, 512], F32)) for i in range(2)]
                rec = [ee(nc.sbuf_tensor(f"rec{i}", [128, 512], F32)) for i in range(2)]
                xnsb = [ee(nc.sbuf_tensor(f"xnsb{i}", [128, 1024], F32)) for i in range(2)]
                xnbfs = [ee(nc.sbuf_tensor(f"xnbfs{i}", [128, 1024], BF16)) for i in range(2)]
                smallc = ee(nc.sbuf_tensor("smallc", [128, 16], F32))
                junk = ee(nc.sbuf_tensor("junk", [128, 1024], F32))
                pw = [ee(nc.psum_tensor(f"pw{i}", [128, 1024], F32)) for i in range(3)]
                pO = [ee(nc.psum_tensor(f"pO{i}", [128, 512], F32)) for i in range(2)]

                wp_ld = 0
                for fc in range(8):
                    for od in range(2):
                        wp_ld = s.emit("sync", lambda e, fc=fc, od=od: e.dma_start_transpose(
                            wprojT[:, fc * 1024 + od * 512: fc * 1024 + (od + 1) * 512],
                            wprojbf_d[od * 512:(od + 1) * 512, fc * 128:(fc + 1) * 128]), 16)

                CS = C["cshift"]
                ktile = lambda hh: hh if hh < 8 else 8 + (hh - 8) // 2

                ex_w, avb_w, nm_w, dn_w, bc_w, avh_w = {}, {}, {}, {}, {}, {}
                kd_w, kd_of_head, qk_last, ln_x_use = {}, {}, {}, {}
                xnst_w, rsb_w = {}, {}
                pjv_list = []
                norm_done = set()
                gbi = 0
                kdseq = 0

                def emit_normalize(idx):
                    if idx in norm_done:
                        return
                    norm_done.add(idx)
                    h_ = idx % 16
                    op = idx % 2
                    s.wait("tensor", "scalar", dn_w[idx])
                    s.wait("tensor", "vector", nm_w.get(idx - 2, 0))
                    bc_w[idx] = s.emit("tensor", lambda e, op=op: e.matmul(
                        pO[op][64:128, 0:512], onf[0:1, 0:64], den[op][:],
                        start=True, stop=True), 1)
                    s.wait("vector", "tensor", bc_w[idx])
                    rw = s.emit("vector", lambda e, op=op: e.reciprocal(
                        rec[op][0:64, :], pO[op][64:128, 0:512]), 1)
                    s.wait("vector", "vector", rw)
                    fcol = (h_ // 2) * 512
                    rbase = (h_ % 2) * 64
                    nm_w[idx] = s.emit("vector", lambda e, op=op, fc=fcol, rb=rbase:
                                       e.tensor_mul(afm[rb:rb + 64, fc:fc + 512],
                                                    pO[op][0:64, 0:512], rec[op][0:64, :]), 1)

                def emit_av(bi0, bat0, pp0, mk0, idx, h, first_av):
                    op = idx % 2
                    s.wait("tensor", "scalar", ex_w[bi0])
                    if mk0 is not None:
                        s.wait("tensor", "vector", mk0)
                    if first_av:
                        s.wait("tensor", "vector", nm_w.get(idx - 2, 0))
                    if bat0[0] == "pair":
                        for sl, kc in enumerate(bat0[1]):
                            avb_w[bi0] = s.emit(
                                "tensor",
                                lambda e, kc=kc, sl=sl, p=pp0, op=op, h=h,
                                st=(first_av and sl == 0):
                                e.matmul(pO[op][0:65, 0:512],
                                         vaug[:, kc * 1040 + 65 * h: kc * 1040 + 65 * h + 65],
                                         probs[p][:, sl * 512:(sl + 1) * 512],
                                         start=st, stop=False),
                                1 if sl == 1 else 0)
                    else:
                        kc, jj = bat0[1], bat0[2]
                        nn = 512 - 128 * jj
                        avb_w[bi0] = s.emit(
                            "tensor",
                            lambda e, kc=kc, jj=jj, nn=nn, p=pp0, op=op, h=h, st=first_av:
                            e.matmul(pO[op][0:65, 128 * jj:128 * jj + nn],
                                     vaug[:, kc * 1040 + 65 * h: kc * 1040 + 65 * h + 65],
                                     probs[p][:, 0:nn],
                                     start=st, stop=(jj == 3)), 1)
                    return False

                for qb in range(NQB):
                    s.wait("sync", "tensor", qk_last.get(qb - 1, 0))
                    qld = 0
                    for h in range(8):
                        qld = s.emit("sync", lambda e, qb=qb, h=h: e.dma_start(
                            out=qa[0:64, h * 512:(h + 1) * 512],
                            in_=qsp_d[qb, 64 * h:64 * (h + 1), :]), 16)
                        qld = s.emit("sync", lambda e, qb=qb, h=h: e.dma_start(
                            out=qa[64:128, h * 512:(h + 1) * 512],
                            in_=ut_d[64 * h:64 * (h + 1), qb * 512:(qb + 1) * 512]), 16)
                    for i in range(4):
                        qld = s.emit("sync", lambda e, qb=qb, i=i: e.dma_start(
                            out=qa[:, (8 + i) * 512:(9 + i) * 512],
                            in_=qsp_d[qb, 512 + 128 * i:512 + 128 * (i + 1), :]), 16)
                    s.wait("sync", "vector", ln_x_use.get(qb - 1, 0))
                    xld = 0
                    for mt in range(4):
                        xld = s.emit("sync", lambda e, qb=qb, mt=mt: e.dma_start(
                            out=xres[:, mt * 1024:(mt + 1) * 1024],
                            in_=x_d[qb * 512 + mt * 128: qb * 512 + (mt + 1) * 128, :]), 16)

                    for h in range(16):
                        idx = 16 * qb + h
                        if h < 8 or (h % 2 == 0):
                            myseq = kdseq
                            if myseq >= 2:
                                s.wait("gpsimd", "tensor",
                                       avh_w.get(kd_of_head.get(myseq - 2, -1), 0))
                            kd_w[myseq] = s.emit("gpsimd", lambda e, qb=qb, h=h, p=myseq % 2:
                                                 e.dma_start(out=kh[p][:, 0:(qb + 1) * 512],
                                                             in_=ksp_d[ktile(h), :, 0:(qb + 1) * 512]),
                                                 16)
                            kdseq += 1
                        khpar = (kdseq - 1) % 2
                        kd_of_head[kdseq - 1] = idx

                        if idx >= 2:
                            emit_normalize(idx - 2)
                        if idx == 11 and dbg and "p2h0" in dbg:
                            a01_d = nc.dram_tensor("afm01dbg", [128, 512], BF16,
                                                   kind="ExternalOutput").ap()
                            a89_d = nc.dram_tensor("afm89dbg", [128, 512], BF16,
                                                   kind="ExternalOutput").ap()
                            s.wait("gpsimd", "vector", nm_w[9])
                            s.emit("gpsimd", lambda e: e.dma_start(out=a01_d, in_=afm[:, 0:512]), 16)
                            s.emit("gpsimd", lambda e: e.dma_start(out=a89_d, in_=afm[:, 4 * 512:5 * 512]), 16)
                        if idx == 1 and dbg and "p2h0" in dbg:
                            afm0_d = nc.dram_tensor("afm0dbg", [64, 512], BF16,
                                                    kind="ExternalOutput").ap()
                            den0_d = nc.dram_tensor("dendbg", [1, 512], F32,
                                                    kind="ExternalOutput").ap()
                            qa0_d = nc.dram_tensor("qadbg", [128, 512], BF16,
                                                   kind="ExternalOutput").ap()
                            kh0_d = nc.dram_tensor("khdbg", [128, 512], BF16,
                                                   kind="ExternalOutput").ap()
                            s.wait("gpsimd", "vector", nm_w[0])
                            s.emit("gpsimd", lambda e: e.dma_start(out=afm0_d, in_=afm[0:64, 0:512]), 16)
                            s.emit("gpsimd", lambda e: e.dma_start(out=den0_d, in_=den[0][:]), 16)
                            s.emit("gpsimd", lambda e: e.dma_start(out=qa0_d, in_=qa[:, 0:512]), 16)
                            s.emit("gpsimd", lambda e: e.dma_start(out=kh0_d, in_=kh[0][:, 0:512]), 16)

                        batches = [("pair", (2 * p, 2 * p + 1)) for p in range(2 * qb)]
                        batches += [("diag", 4 * qb + jj, jj) for jj in range(4)]
                        pend = []
                        first_av = True
                        for bat in batches:
                            pwpar = gbi % 3
                            s.wait("tensor", "gpsimd", kd_w[kdseq - 1])
                            s.wait("tensor", "sync", qld)
                            s.wait("tensor", "scalar", ex_w.get(gbi - 3, 0))
                            qab = ktile(h) * 512
                            rb = 0 if h < 8 else (h % 2) * 64
                            kk = 128 if h < 8 else 64
                            if bat[0] == "pair":
                                lq = None
                                for sl, kc in enumerate(bat[1]):
                                    lq = s.emit("tensor",
                                                lambda e, kc=kc, sl=sl, p=pwpar, rb=rb, kk=kk,
                                                qb0=qab, kp=khpar:
                                                e.matmul(pw[p][:, sl * 512:(sl + 1) * 512],
                                                         kh[kp][rb:rb + kk, kc * 128:(kc + 1) * 128],
                                                         qa[rb:rb + kk, qb0:qb0 + 512],
                                                         start=True, stop=True),
                                                1 if sl == 1 else 0)
                                cols = (0, 1024)
                            else:
                                kc, jj = bat[1], bat[2]
                                nn = 512 - 128 * jj
                                lq = s.emit("tensor",
                                            lambda e, kc=kc, nn=nn, jj=jj, p=pwpar, rb=rb, kk=kk,
                                            qb0=qab, kp=khpar:
                                            e.matmul(pw[p][:, 0:nn],
                                                     kh[kp][rb:rb + kk, kc * 128:(kc + 1) * 128],
                                                     qa[rb:rb + kk, qb0 + 128 * jj:qb0 + 512],
                                                     start=True, stop=True), 1)
                                cols = (0, nn)
                            s.wait("scalar", "tensor", lq)
                            s.wait("scalar", "tensor", avb_w.get(gbi - 3, 0))
                            if h < 8:
                                ex_w[gbi] = s.emit("scalar",
                                                   lambda e, p=pwpar, c=cols, b=-CS[h]:
                                                   e.activation(probs[p][:, c[0]:c[1]],
                                                                pw[p][:, c[0]:c[1]],
                                                                AF.Exp, bias=np.float32(b)), 1)
                            else:
                                ex_w[gbi] = s.emit("scalar",
                                                   lambda e, p=pwpar, c=cols, hh=h - 8:
                                                   e.activation(probs[p][:, c[0]:c[1]],
                                                                pw[p][:, c[0]:c[1]],
                                                                AF.Exp, bias=c8b[:, hh:hh + 1]), 1)
                            mk = None
                            if bat[0] == "diag":
                                s.wait("vector", "scalar", ex_w[gbi])
                                mk = s.emit("vector", lambda e, p=pwpar: e.tensor_mul(
                                    probs[p][:, 0:128], probs[p][:, 0:128], msk[:]), 1)
                            pend.append((gbi, bat, pwpar, mk))
                            gbi += 1
                            if len(pend) >= 2:
                                a0 = pend.pop(0)
                                first_av = emit_av(a0[0], a0[1], a0[2], a0[3], idx, h, first_av)
                        while pend:
                            a0 = pend.pop(0)
                            first_av = emit_av(a0[0], a0[1], a0[2], a0[3], idx, h, first_av)
                        avh_w[idx] = s.val("tensor")
                        qk_last[qb] = s.val("tensor")
                        s.wait("scalar", "tensor", avh_w[idx])
                        s.wait("scalar", "tensor", bc_w.get(idx - 2, 0))
                        s.emit("scalar", lambda e, op=idx % 2: e.activation(
                            den[op][:], pO[op][64:65, 0:512], AF.Copy), 1)
                        dn_w[idx] = s.fence("scalar")

                    emit_normalize(16 * qb + 14)
                    emit_normalize(16 * qb + 15)

                    s.wait("tensor", "vector", nm_w[16 * qb + 15])
                    s.wait("tensor", "sync", xld)
                    s.wait("tensor", "scalar", ex_w[gbi - 1])
                    for p in range(8):
                        mt, ncol = divmod(p, 2)
                        slot = p % 4
                        pwt, lohi = pw[slot // 2], slot % 2
                        if len(pjv_list) >= 4:
                            s.wait("tensor", "vector", pjv_list[-4])
                        lastp = None
                        for fc in range(8):
                            lastp = s.emit("tensor",
                                           lambda e, fc=fc, mt=mt, nn=ncol, t=pwt, lh=lohi:
                                           e.matmul(t[:, lh * 512:(lh + 1) * 512],
                                                    afm[:, fc * 512 + mt * 128:
                                                        fc * 512 + (mt + 1) * 128],
                                                    wprojT[:, fc * 1024 + nn * 512:
                                                           fc * 1024 + (nn + 1) * 512],
                                                    start=(fc == 0), stop=(fc == 7)),
                                           1 if fc == 7 else 0)
                        s.wait("vector", "tensor", lastp)
                        s.wait("vector", "scalar", rsb_w.get(mt % 2, 0))
                        s.emit("vector", lambda e, mt=mt, nn=ncol, t=pwt, lh=lohi:
                               e.scalar_tensor_tensor(
                                   rsb[mt % 2][:, nn * 512:(nn + 1) * 512],
                                   t[:, lh * 512:(lh + 1) * 512], 1.0,
                                   xres[:, mt * 1024 + nn * 512:mt * 1024 + (nn + 1) * 512],
                                   ALU.mult, ALU.add), 1)
                        pjv_list.append(s.emit("vector", lambda e, mt=mt, nn=ncol:
                                               e.tensor_add(
                                                   rsb[mt % 2][:, nn * 512:(nn + 1) * 512],
                                                   rsb[mt % 2][:, nn * 512:(nn + 1) * 512],
                                                   bpbc[:, nn * 512:(nn + 1) * 512]), 1))
                        if ncol == 1:
                            mpar = mt % 2
                            if qb == 0 and mt == 0 and dbg and "p2h0" in dbg:
                                rsb_d = nc.dram_tensor("rsbdbg", [128, 1024], F32,
                                                       kind="ExternalOutput").ap()
                                s.wait("gpsimd", "vector", pjv_list[-1])
                                s.emit("gpsimd", lambda e: e.dma_start(
                                    out=rsb_d, in_=rsb[0][:, 0:1024]), 16)
                            s.wait("scalar", "vector", pjv_list[-1])
                            s.wait("scalar", "gpsimd", xnst_w.get(mpar, 0))
                            lnwm, _ = emit_layernorm(s, rsb[mpar][:, 0:1024],
                                                     xnsb[mpar][:, 0:1024],
                                                     smallc, 8 * mpar, junk[:, 0:1024], 1024)
                            rsb_w[mpar] = lnwm
                            if qb == 0 and mt == 0 and dbg and "p2h0" in dbg:
                                xnsb_d = nc.dram_tensor("xnsbdbg", [128, 1024], F32,
                                                        kind="ExternalOutput").ap()
                                s.wait("gpsimd", "scalar", lnwm)
                                s.emit("gpsimd", lambda e: e.dma_start(
                                    out=xnsb_d, in_=xnsb[0][:, 0:1024]), 16)
                            cast_ = s.emit("scalar", lambda e, mp=mpar: e.activation(
                                xnbfs[mp][:], xnsb[mp][:], AF.Copy), 1)
                            s.wait("gpsimd", "scalar", cast_)
                            row0 = qb * 512 + mt * 128
                            s.emit("gpsimd", lambda e, r=row0, mp=mpar: e.dma_start(
                                out=xn32_d[r:r + 128, :], in_=xnsb[mp][:]), 16)
                            xnst_w[mpar] = s.emit("gpsimd", lambda e, r=row0, mp=mpar:
                                                  e.dma_start(out=xnbf_d[r:r + 128, :],
                                                              in_=xnbfs[mp][:]), 16)
                    ln_x_use[qb] = s.val("vector")

                s.barrier()
                with nc.named_scope("P2"):
                    s.flush(block)

        if dbg_stop == "p2":
            s.wait("sync", "gpsimd", s.val("gpsimd"))
            s.flush(block)
            return nc

        with ExitStack() as SP:
            ee = SP.enter_context
            wfcT = ee(nc.sbuf_tensor("wfcT", [128, 8 * 4096], BF16))
            w2T = ee(nc.sbuf_tensor("w2T", [128, 32 * 1024], BF16))
            gfm = ee(nc.sbuf_tensor("gfm", [128, 32 * 256], BF16))
            gq = [ee(nc.sbuf_tensor(f"gq{i}", [128, 256], BF16)) for i in range(4)]
            xnfm = [ee(nc.sbuf_tensor("xnfm0", [128, 8 * 256], BF16))]
            xn32b = [ee(nc.sbuf_tensor("xn32b0", [128, 2048], F32))]
            rsb3 = [ee(nc.sbuf_tensor(f"rsb3{i}", [128, 1024], F32)) for i in range(2)]
            outsb = [ee(nc.sbuf_tensor(f"outsb{i}", [128, 1024], F32)) for i in range(2)]
            stsb = ee(nc.sbuf_tensor("stsb", [1, 512], F32))
            stc = ee(nc.sbuf_tensor("stc", [128, 16], F32))
            mhi = ee(nc.sbuf_tensor("mhi", [1, 256], BF16))
            mlo = ee(nc.sbuf_tensor("mlo", [1, 256], BF16))
            w2hi = ee(nc.sbuf_tensor("w2hi", [1, 1024], BF16))
            w2lo = ee(nc.sbuf_tensor("w2lo", [1, 1024], BF16))
            mr32a = ee(nc.sbuf_tensor("mr32a", [1, 1024], F32))
            mr32b = ee(nc.sbuf_tensor("mr32b", [1, 1024], F32))
            lnbbf = ee(nc.sbuf_tensor("lnbbf", [128, 32], BF16))
            cbc = ee(nc.sbuf_tensor("cbc", [128, 1024], F32))
            smallc3 = ee(nc.sbuf_tensor("smallc3", [128, 16], F32))
            ph = [ee(nc.psum_tensor(f"ph{i}", [128, 512], F32)) for i in range(2)]
            pp = [ee(nc.psum_tensor(f"pp{i}", [128, 1024], F32)) for i in range(2)]
            pst = ee(nc.psum_tensor("pst", [128, 512], F32))
            ptr1 = ee(nc.psum_tensor("ptr1", [128, 512], F32))

            ld_w2 = 0
            for hc in range(32):
                ld_w2 = s.emit("sync", lambda e, hc=hc: e.dma_start_transpose(
                    w2T[:, hc * 1024:(hc + 1) * 1024],
                    wprq_d[0:1024, hc * 128:(hc + 1) * 128]), 16)
            ld_wf = 0
            for dc in range(8):
                for hb in range(8):
                    ld_wf = s.emit("sync", lambda e, dc=dc, hb=hb: e.dma_start_transpose(
                        wfcT[:, dc * 4096 + hb * 512: dc * 4096 + (hb + 1) * 512],
                        wfcq_d[hb * 512:(hb + 1) * 512, dc * 128:(dc + 1) * 128]), 16)

            s.emit("scalar", lambda e: e.activation(lnbbf[:], lnbc[:], AF.Copy), 1)
            sc0 = s.fence("scalar")
            s.wait("tensor", "sync", ld_w2)
            s.wait("tensor", "scalar", sc0)
            tcd = None
            for ncol in range(2):
                for hc in range(32):
                    tcd = s.emit("tensor", lambda e, hc=hc, nn=ncol: e.matmul(
                        pp[0][0:1, nn * 512:(nn + 1) * 512], lnbbf[:, hc:hc + 1],
                        w2T[:, hc * 1024 + nn * 512: hc * 1024 + (nn + 1) * 512],
                        start=(hc == 0), stop=(hc == 31)), 1 if hc == 31 else 0)
            s.wait("scalar", "tensor", tcd)
            s.emit("scalar", lambda e: e.activation(mr32a[:], pp[0][0:1, 0:1024], AF.Copy), 1)
            scd = s.fence("scalar")
            sscale = None
            for hc in range(32):
                sscale = s.emit("scalar", lambda e, hc=hc: e.activation(
                    w2T[:, hc * 1024:(hc + 1) * 1024], w2T[:, hc * 1024:(hc + 1) * 1024],
                    AF.Identity, scale=lnwc[:, hc:hc + 1]), 1)
            s.wait("tensor", "scalar", scd)
            for half in range(2):
                tb2 = s.emit("tensor", lambda e, h=half: e.matmul(
                    ptr1[:, 0:512], onf[0:1, 0:128], mr32a[0:1, h * 512:(h + 1) * 512],
                    start=True, stop=True), 1)
                s.wait("scalar", "tensor", tb2)
                scb = s.emit("scalar", lambda e, h=half: e.activation(
                    cbc[:, h * 512:(h + 1) * 512], ptr1[:, 0:512], AF.Copy), 1)
                s.wait("tensor", "scalar", scb)
            s.wait("tensor", "scalar", sscale)
            tws = None
            for ncol in range(2):
                for hc in range(32):
                    tws = s.emit("tensor", lambda e, hc=hc, nn=ncol: e.matmul(
                        pp[0][0:1, nn * 512:(nn + 1) * 512], onb[:, 0:1],
                        w2T[:, hc * 1024 + nn * 512: hc * 1024 + (nn + 1) * 512],
                        start=(hc == 0), stop=(hc == 31)), 1 if hc == 31 else 0)
            s.wait("scalar", "tensor", tws)
            s.chain("scalar", lambda e: e.activation(mr32a[:], pp[0][0:1, 0:1024], AF.Copy))
            scd2 = s.fence("scalar")
            s.chain("scalar", lambda e: e.activation(w2hi[:], mr32a[:], AF.Copy))
            sw2h = s.chain("scalar", lambda e: e.activation(mr32b[:], w2hi[:], AF.Copy))
            sw2h = s.fence("scalar")
            s.wait("vector", "scalar", sw2h)
            vw2c = s.emit("vector", lambda e: e.scalar_tensor_tensor(
                w2lo[:], mr32a[:], 1.0, mr32b[:], ALU.mult, ALU.subtract), 1)

            projev_w = {}
            fc_last = {}
            proj_last = {}
            blk3_done = {}
            xn_use = {}
            ln_vred = {}
            outst_w = []
            sst_prev = 0
            for blk in range(NP3):
                bp = blk % 2
                s.wait("sync", "tensor", fc_last.get(blk - 1, 0))
                xf3 = 0
                for dc in range(8):
                    xf3 = s.emit("sync", lambda e, blk=blk, dc=dc: e.dma_start_transpose(
                        xnfm[0][:, dc * 256:(dc + 1) * 256],
                        xnbf_d[blk * 256:(blk + 1) * 256, dc * 128:(dc + 1) * 128]), 16)
                s.wait("sync", "vector", xn_use.get(blk - 1, 0))
                xl3 = 0
                for mt in range(2):
                    xl3 = s.emit("sync", lambda e, blk=blk, mt=mt: e.dma_start(
                        out=xn32b[0][:, mt * 1024:(mt + 1) * 1024],
                        in_=xn32_d[blk * 256 + mt * 128: blk * 256 + (mt + 1) * 128, :]), 16)

                s.wait("tensor", "sync", max(xf3, ld_wf))
                s.wait("tensor", "vector", vw2c)
                s.wait("scalar", "tensor", proj_last.get(blk - 1, 0))
                ge_w = {}
                geP = {}
                gqP = {}
                LAG = 3

                def emit_stats(hcs):
                    s.wait("tensor", "scalar", ge_w[hcs])
                    if hcs == 0:
                        s.wait("tensor", "scalar", sst_prev)
                    s.emit("tensor", lambda e, hcs=hcs: e.matmul(
                        pst[0:1, 0:256], onb[:, 0:1], gfm[:, hcs * 256:(hcs + 1) * 256],
                        start=(hcs == 0), stop=False), 0)
                    s.wait("tensor", "vector", gqP[hcs % 4])
                    s.emit("tensor", lambda e, hcs=hcs: e.matmul(
                        pst[0:1, 256:512], onb[:, 0:1], gq[hcs % 4][:, 0:256],
                        start=False, stop=(hcs == 31)), 1 if hcs == 31 else 0)

                for hc in range(32):
                    par = hc % 2
                    if hc >= LAG:
                        emit_stats(hc - LAG)
                    s.wait("tensor", "scalar", geP.get(par, 0))
                    lastf = None
                    for dc in range(8):
                        lastf = s.emit("tensor", lambda e, dc=dc, hc=hc, q=par: e.matmul(
                            ph[q][:, 0:256],
                            wfcT[:, dc * 4096 + hc * 128: dc * 4096 + (hc + 1) * 128],
                            xnfm[0][:, dc * 256:(dc + 1) * 256],
                            start=(dc == 0), stop=(dc == 7)), 1 if dc == 7 else 0)
                    s.wait("scalar", "tensor", lastf)
                    ge_w[hc] = s.emit("scalar", lambda e, hc=hc, q=par: e.activation(
                        gfm[:, hc * 256:(hc + 1) * 256], ph[q][:, 0:256], AF.Gelu), 1)
                    geP[par] = ge_w[hc]
                    s.wait("vector", "scalar", ge_w[hc])
                    if hc >= 4:
                        s.wait("vector", "tensor", s.val("tensor") if False else 0)
                    gqP[hc % 4] = s.emit("vector", lambda e, hc=hc: e.tensor_mul(
                        gq[hc % 4][:, 0:256], gfm[:, hc * 256:(hc + 1) * 256],
                        gfm[:, hc * 256:(hc + 1) * 256]), 1)
                fc_last[blk] = s.val("tensor")
                for hcs in range(32 - LAG, 32):
                    emit_stats(hcs)
                tst = s.val("tensor")
                projmain = {}
                for mt in range(2):
                    s.wait("tensor", "scalar", projev_w.get((blk - 1, mt), 0))
                    for hc in range(32):
                        for ncol in range(2):
                            s.emit("tensor", lambda e, hc=hc, nn=ncol, mt=mt: e.matmul(
                                pp[mt][:, nn * 512:(nn + 1) * 512],
                                gfm[:, hc * 256 + mt * 128: hc * 256 + (mt + 1) * 128],
                                w2T[:, hc * 1024 + nn * 512: hc * 1024 + (nn + 1) * 512],
                                start=(hc == 0), stop=False), 0)
                    projmain[mt] = s.val("tensor")
                s.wait("scalar", "tensor", tst)
                s.emit("scalar", lambda e: e.activation(stsb[:], pst[0:1, 0:512], AF.Copy), 1)
                sst = s.fence("scalar")
                sst_prev = sst
                s.wait("tensor", "scalar", sst)
                ttr = None
                for k in range(4):
                    ttr = s.emit("tensor", lambda e, k=k: e.transpose(
                        ptr1[:, k:k + 1], stsb[0:1, k * 128:(k + 1) * 128], idf[0:1, 0:1]), 1)
                s.wait("scalar", "tensor", ttr)
                s.emit("scalar", lambda e: e.activation(stc[:, 0:4], ptr1[:, 0:4], AF.Copy), 1)
                s.wait("scalar", "tensor", proj_last.get(blk - 1, 0))
                s.chain("scalar", lambda e: e.activation(
                    mr32a[0:1, 0:256], stsb[0:1, 0:256], AF.Copy, scale=1.0 / FH))
                s.chain("scalar", lambda e: e.activation(
                    mhi[:], mr32a[0:1, 0:256], AF.Copy, scale=-1.0))
                smh32 = s.chain("scalar", lambda e: e.activation(
                    mr32b[0:1, 0:256], mhi[:], AF.Copy))
                smh = s.fence("scalar")
                s.wait("vector", "scalar", smh32)
                s.wait("vector", "tensor", proj_last.get(blk - 1, 0))
                vml2 = s.emit("vector", lambda e: e.scalar_tensor_tensor(
                    mlo[:], mr32a[0:1, 0:256], -1.0, mr32b[0:1, 0:256],
                    ALU.mult, ALU.subtract), 1)
                for mt in range(2):
                    b = 4 + 4 * mt
                    s.chain("scalar", lambda e, mt=mt, b=b: e.activation(
                        stc[:, b:b + 1], stc[:, mt:mt + 1], AF.Copy, scale=1.0 / FH))
                    s.chain("scalar", lambda e, b=b: e.activation(
                        stc[:, b + 1:b + 2], stc[:, b:b + 1], AF.Copy, scale=-1.0))
                    s.chain("scalar", lambda e, b=b: e.activation(
                        stc[:, b + 2:b + 3], stc[:, b:b + 1], AF.Identity,
                        scale=stc[:, b + 1:b + 2]))
                    s.chain("scalar", lambda e, b=b: e.activation(
                        stc[:, b + 2:b + 3], stc[:, b + 2:b + 3], AF.Identity,
                        bias=np.float32(LN_EPS)))
                    s.chain("scalar", lambda e, mt=mt, b=b: e.activation(
                        stc[:, b + 3:b + 4], stc[:, 2 + mt:3 + mt], AF.Ln,
                        scale=1.0 / FH, bias=stc[:, b + 2:b + 3]))
                    s.chain("scalar", lambda e, b=b: e.activation(
                        stc[:, b + 3:b + 4], stc[:, b + 3:b + 4], AF.Exp, scale=-0.5))
                rstd_done = s.val("scalar")

                for mt in range(2):
                    s.wait("tensor", "scalar", smh)
                    s.wait("tensor", "vector", vml2)
                    lastp = None
                    for ncol in range(2):
                        for mrow, wrow in ((mhi, w2hi), (mhi, w2lo), (mlo, w2hi), (mlo, w2lo)):
                            lastp = s.emit("tensor", lambda e, nn=ncol, mt=mt, mr=mrow, wr=wrow:
                                           e.matmul(pp[mt][:, nn * 512:(nn + 1) * 512],
                                                    mr[0:1, mt * 128:(mt + 1) * 128],
                                                    wr[0:1, nn * 512:(nn + 1) * 512],
                                                    start=False, stop=(mr is mlo and wr is w2lo)),
                                           1 if (mrow is mlo and wrow is w2lo) else 0)
                    proj_last[blk] = lastp
                    s.wait("scalar", "tensor", lastp)
                    s.wait("scalar", "vector", ln_vred.get(mt, 0))
                    pev = s.emit("scalar", lambda e, mt=mt, b=4 + 4 * mt: e.activation(
                        rsb3[mt][:, 0:1024], pp[mt][:, 0:1024], AF.Identity,
                        scale=stc[:, b + 3:b + 4]), 1)
                    projev_w[(blk, mt)] = pev
                    s.wait("vector", "scalar", pev)
                    s.wait("vector", "sync", xl3)
                    for ncol in range(2):
                        s.emit("vector", lambda e, mt=mt, nn=ncol: e.tensor_add(
                            rsb3[mt][:, nn * 512:(nn + 1) * 512],
                            rsb3[mt][:, nn * 512:(nn + 1) * 512],
                            xn32b[0][:, mt * 1024 + nn * 512: mt * 1024 + (nn + 1) * 512]), 1)
                    vr = None
                    for ncol in range(2):
                        vr = s.emit("vector", lambda e, mt=mt, nn=ncol: e.tensor_add(
                            rsb3[mt][:, nn * 512:(nn + 1) * 512],
                            rsb3[mt][:, nn * 512:(nn + 1) * 512],
                            cbc[:, nn * 512:(nn + 1) * 512]), 1)
                    s.wait("scalar", "vector", vr)
                    if len(outst_w) >= 2:
                        s.wait("scalar", "gpsimd", outst_w[-2])
                    lnwm, vred = emit_layernorm(s, rsb3[mt][:, 0:1024], outsb[mt][:, 0:1024],
                                                smallc3, 8 * mt, outsb[mt][:, 0:1024], 1024)
                    ln_vred[mt] = vred
                    s.wait("gpsimd", "scalar", lnwm)
                    row0 = blk * 256 + mt * 128
                    outst_w.append(s.emit("gpsimd", lambda e, r=row0, mt=mt: e.dma_start(
                        out=out_d[r:r + 128, :], in_=outsb[mt][:]), 16))
                xn_use[blk] = s.val("vector")
                blk3_done[blk] = s.val("tensor")

            s.wait("sync", "gpsimd", s.val("gpsimd"))
            with nc.named_scope("P3"):
                s.flush(block)

    return nc


_NC = None


def kernel(**inputs):
    global _NC
    C = _host_consts()
    if _NC is None:
        _NC = build()
    nc = _NC
    x = np.ascontiguousarray(np.asarray(inputs["L0_attn_x"], dtype=np.float32))
    base = dict(
        w_qkv=np.ascontiguousarray(np.asarray(inputs["L0_attn_w_qkv"], dtype=np.float32)),
        b_qkv=np.ascontiguousarray(np.asarray(inputs["L0_attn_b_qkv"], dtype=np.float32)),
        w_proj=np.ascontiguousarray(np.asarray(inputs["L0_attn_w_proj"], dtype=np.float32)),
        b_proj=np.ascontiguousarray(np.asarray(inputs["L0_attn_b_proj"], dtype=np.float32)),
        pos_ind=np.ascontiguousarray(np.asarray(inputs["L0_attn_pos_independent"], dtype=np.float32)),
        W_fc=np.ascontiguousarray(np.asarray(inputs["L0_ffn_W_fc"], dtype=np.float32)),
        ln_w=np.ascontiguousarray(np.asarray(inputs["L0_ffn_ln_w"], dtype=np.float32)),
        ln_b=np.ascontiguousarray(np.asarray(inputs["L0_ffn_ln_b"], dtype=np.float32)),
        W_pr=np.ascontiguousarray(np.asarray(inputs["L0_ffn_W_proj"], dtype=np.float32)),
        u_tab=C["u_tab"], wb_tab=C["wb_tab"],
        ident_f32=C["ident_f32"], ones_f32=C["ones_f32"], ones_bf=C["ones_bf"],
        mask_bf=C["mask_bf"],
    )
    in_maps = [dict(base, x=x[b]) for b in range(B)]
    res = run_bass_kernel_spmd(nc, in_maps, core_ids=list(range(B)))
    return np.stack([res.results[b]["out"] for b in range(B)], axis=0)
